# version 12 (final)
# baseline (speedup 1.0000x reference)
import math
import contextlib
import numpy as np

import concourse.bass as bass
import concourse.mybir as mybir
from concourse.bass_utils import run_bass_kernel_spmd

F32 = mybir.dt.float32
BF16 = mybir.dt.bfloat16
AF = mybir.ActivationFunctionType
ALU = mybir.AluOpType
AX = mybir.AxisListType

MAGIC = float(1.5 * 2 ** 23)
INV2PI = float(1.0 / (2.0 * math.pi))
C1 = 6.28125
C2 = float(2.0 * math.pi - 6.28125)
PIH = float(math.pi / 2.0)
TAU_INV = float(1.0 / math.sqrt(128.0))
NEG_BIG = -3.0e38
MASK_PEN = -1.0e30

D = 1024
K = 128
V = 2048
I3 = 3072
CH = 256
KT = 512
SEGT = 4


def cfg_for(T):
    c = {}
    c["T"] = T
    c["OWN"] = T // 4
    c["NP"] = c["OWN"] // CH
    c["NOG"] = c["NP"] // 2
    c["NG"] = T // KT
    c["NCT"] = T // 128
    return c


def rank_of(p, c):
    return 4 * p + (c if p % 2 == 0 else 3 - c)


class Prog:
    def __init__(self, nc):
        self.nc = nc
        self.es = contextlib.ExitStack()
        self.esem = {}
        for e in ("sync", "scalar", "vector", "tensor", "gpsimd"):
            self.esem[e] = self.es.enter_context(nc.semaphore("se_" + e))
        self.einc = {e: 0 for e in self.esem}
        self.chan = {}
        self.spec_tot = {}
        self.ops = []
        self.phase_idx = 0

    def add_chan(self, name, engine):
        sem = self.es.enter_context(self.nc.semaphore("ch_" + name))
        self.chan[name] = [sem, engine, 0]

    def op_inc(self, eng, fn, inc_sem, inc_val, reads=(), writes=()):
        tot = self.spec_tot.get(id(inc_sem), 0) + inc_val
        self.spec_tot[id(inc_sem)] = tot
        self.ops.append(dict(eng=eng, fn=fn, reads=tuple(reads), writes=tuple(writes),
                             chan=None, extra=(), force=False, w=None,
                             inc_spec=(inc_sem, inc_val, tot)))

    def op(self, eng, fn, reads=(), writes=(), extra_waits=(), force_inc=False,
           w=None):
        self.ops.append(dict(eng=eng, fn=fn, reads=tuple(reads), writes=tuple(writes),
                             chan=None, extra=tuple(extra_waits), force=force_inc,
                             w=w, inc_spec=None))

    def dma(self, chan, fn, reads=(), writes=()):
        eng = self.chan[chan][1]
        self.ops.append(dict(eng=eng, fn=fn, reads=tuple(reads), writes=tuple(writes),
                             chan=chan, extra=(), force=False, inc_spec=None))

    def fence(self):
        pend = {}
        for o in self.ops:
            if o["chan"] is not None:
                pend[o["chan"]] = pend.get(o["chan"], self.chan[o["chan"]][2]) + 1
        for cn, cnt in pend.items():
            sem, eng, _ = self.chan[cn]
            self.op(eng, lambda e: e.nop(), extra_waits=((sem, 16 * cnt),))

    def end_phase(self):
        self.fence()
        ops = self.ops
        lastw, readers = {}, {}
        for i, o in enumerate(ops):
            deps = set()
            for k in o["reads"]:
                if k in lastw:
                    deps.add(lastw[k])
            for k in o["writes"]:
                if k in lastw:
                    deps.add(lastw[k])
                for r in readers.get(k, ()):
                    deps.add(r)
            dd = []
            for d in deps:
                p = ops[d]
                if p["chan"] is None and p["eng"] == o["eng"]:
                    pw = p.get("w")
                    hazard = (pw is not None and pw < 512 and
                              (set(p["writes"]) &
                               (set(o["reads"]) | set(o["writes"]))))
                    if not hazard:
                        continue
                dd.append(d)
            o["deps"] = sorted(dd)
            for k in o["writes"]:
                lastw[k] = i
                readers[k] = []
            for k in o["reads"]:
                readers.setdefault(k, []).append(i)
        for o in ops:
            if o["chan"] is not None:
                self.chan[o["chan"]][2] += 1
                o["chan_idx"] = self.chan[o["chan"]][2]
        for o in ops:
            o.setdefault("need_inc", False)
            if o["force"]:
                o["need_inc"] = True
        for o in ops:
            for d in o["deps"]:
                if ops[d]["chan"] is None and not ops[d].get("inc_spec"):
                    ops[d]["need_inc"] = True
        for o in ops:
            if o["chan"] is None and o["need_inc"]:
                self.einc[o["eng"]] += 1
                o["inc_val"] = self.einc[o["eng"]]
        with self.nc.Block("ph%d" % self.phase_idx) as block:
            for ename in ("sync", "scalar", "vector", "tensor", "gpsimd"):
                mine = [o for o in ops if o["eng"] == ename]
                if not mine:
                    continue
                prog = self

                def emit(eng, mine=mine, prog=prog):
                    satisfied = {}
                    for o in mine:
                        waits = []
                        for d in o["deps"]:
                            p = prog.ops_all[d]
                            if p["chan"] is not None:
                                sem = prog.chan[p["chan"]][0]
                                waits.append((sem, 16 * p["chan_idx"]))
                            elif p.get("inc_spec"):
                                waits.append((p["inc_spec"][0], p["inc_spec"][2]))
                            else:
                                waits.append((prog.esem[p["eng"]], p["inc_val"]))
                        if o["chan"] is not None and o["chan_idx"] > 1:
                            waits.append((prog.chan[o["chan"]][0], 16 * (o["chan_idx"] - 1)))
                        for sem, val in o["extra"]:
                            waits.append((sem, val))
                        best = {}
                        for sem, val in waits:
                            kk = id(sem)
                            if val > best.get(kk, (None, -1))[1]:
                                best[kk] = (sem, val)
                        for sem, val in best.values():
                            if satisfied.get(id(sem), -1) < val:
                                eng.wait_ge(sem, val)
                                satisfied[id(sem)] = val
                        inst = o["fn"](eng)
                        if o["chan"] is not None:
                            inst.then_inc(prog.chan[o["chan"]][0], 16)
                        elif o.get("inc_spec"):
                            inst.then_inc(o["inc_spec"][0], o["inc_spec"][1])
                        elif o["need_inc"]:
                            inst.then_inc(prog.esem[o["eng"]], 1)

                self.ops_all = ops
                getattr(block, ename)(emit)
        self.ops = []
        self.phase_idx += 1


def build_nc(T):
    cfg = cfg_for(T)
    NP, NOG, NG, NCT, OWN = cfg["NP"], cfg["NOG"], cfg["NG"], cfg["NCT"], cfg["OWN"]
    NT0 = NP + NG

    nc = bass.Bass("TRN2", target_bir_lowering=False)
    nc.num_devices = 8
    dt = {}

    def din(name, shape, dtype=F32):
        dt[name] = nc.dram_tensor(name, shape, dtype, kind="ExternalInput").ap()
        return dt[name]

    x_own = din("x_own", [OWN, D])
    x_cov = din("x_cov", [T, D])
    t0s = din("t0s", [1, NT0])
    maskb = din("maskb", [128, NP * 4])
    omega_c = din("omega_c", [128, 4])
    omega_r = din("omega_r", [1, 512])
    ident_b = din("ident_b", [128, 128], BF16)
    ident_f = din("ident_f", [128, 128])
    gln_i = din("gln", [128, 8])
    xlu_i = din("xlu", [128, 1])
    xlv_i = din("xlv", [128, 1])
    Wq_i = din("Wq", [D, K])
    Wk_i = din("Wk", [D, K])
    Wr_i = din("Wr", [D, K])
    Wv_i = din("Wv", [D, V])
    Wg_i = din("Wg", [D, V])
    Wo_i = din("Wo", [V, D])
    wup_i = din("wup", [I3, D])
    wgate_i = din("wgate", [I3, D])
    wdown_i = din("wdown", [D, I3])
    out_own = nc.dram_tensor("out_own", [OWN, D], F32, kind="ExternalOutput").ap()

    v_scr = nc.dram_tensor("v_scr", [NCT, 128, V], BF16, kind="Internal").ap()
    ct_scr = nc.dram_tensor("ct_scr", [NG, 128, 4096], BF16, kind="Internal").ap()
    qt_scr = nc.dram_tensor("qt_scr", [NP, 128, 9 * 256], BF16, kind="Internal").ap()
    g_scr = nc.dram_tensor("g_scr", [NP, 2, 128, V], BF16, kind="Internal").ap()
    y1_scr = nc.dram_tensor("y1_scr", [NP, 2, 128, D], F32, kind="Internal").ap()
    up_scr = nc.dram_tensor("up_scr", [24, 128, OWN], BF16, kind="Internal").ap()
    wupT_scr = nc.dram_tensor("wupT_scr", [24, 8, 128, 128], BF16, kind="Internal").ap()
    wgT_scr = nc.dram_tensor("wgT_scr", [24, 8, 128, 128], BF16, kind="Internal").ap()
    wdT_scr = nc.dram_tensor("wdT_scr", [24, 128, D], BF16, kind="Internal").ap()
    cc_ins = [nc.dram_tensor("cc_in%d" % i, [1, 1], F32, kind="Internal").ap()
              for i in range(5)]
    cc_outs = [nc.dram_tensor("cc_out%d" % i, [1, 1], F32, kind="Internal").ap()
               for i in range(5)]

    P = Prog(nc)
    ccsem = P.es.enter_context(nc.semaphore("ccsem"))
    for name, eng in [
        ("xcA", "sync"), ("xcB", "sync"), ("vA", "sync"), ("vB", "sync"),
        ("ctA", "sync"), ("ctB", "sync"), ("wl1", "sync"), ("wl2", "sync"),
        ("qtL", "scalar"), ("gL", "scalar"), ("xoL", "scalar"),
        ("y1LA", "scalar"), ("y1LB", "scalar"), ("wuA", "scalar"),
        ("wuB", "scalar"), ("wgA", "scalar"), ("wgB", "scalar"), ("wdl", "scalar"),
        ("vstA", "gpsimd"), ("vstB", "gpsimd"), ("ctsA", "gpsimd"), ("ctsB", "gpsimd"),
        ("qtS", "gpsimd"), ("gS", "gpsimd"), ("y1S", "gpsimd"), ("upS", "gpsimd"),
        ("wtS", "gpsimd"), ("outS", "gpsimd"), ("ccB", "gpsimd"),
        ("upLA", "gpsimd"), ("upLB", "gpsimd"), ("woL", "gpsimd"),
    ]:
        P.add_chan(name, eng)

    es_global = P.es

    def sb(name, shape, dtype, stack):
        return stack.enter_context(nc.sbuf_tensor(name, shape, dtype))

    c_gln = sb("c_gln", [128, 8], F32, es_global)
    c_xlu = sb("c_xlu", [128, 1], F32, es_global)
    c_xlv = sb("c_xlv", [128, 1], F32, es_global)
    c_om = sb("c_om", [128, 4], F32, es_global)
    c_omr = sb("c_omr", [1, 512], F32, es_global)
    c_idb = sb("c_idb", [128, 128], BF16, es_global)
    c_idf = sb("c_idf", [128, 128], F32, es_global)
    c_mask = sb("c_mask", [128, NP * 4], F32, es_global)
    c_ci = sb("c_ci", [128, 512], F32, es_global)
    c_t0 = sb("c_t0", [1, NT0], F32, es_global)
    c_tc = sb("c_tc", [128, 4 * NT0], F32, es_global)
    c_ts = sb("c_ts", [128, 4 * NT0], F32, es_global)
    c_tsn = sb("c_tsn", [128, 4 * NT0], F32, es_global)
    k_full = sb("k_full", [128, T], BF16, es_global)
    amax_y1 = sb("amax_y1", [128, 2], F32, es_global)
    gmx = sb("gmx", [1, 8], F32, es_global)

    ps = [P.es.enter_context(nc.psum_tensor("ps%d" % i, [128, 512], F32))
          for i in range(8)]

    def psb(i, n=128):
        return ps[i][:, :].bitcast(BF16)[:, 0:n]

    es_wp = contextlib.ExitStack()
    Wq_b = sb("Wq_b", [128, 8 * K], BF16, es_wp)
    Wk_b = sb("Wk_b", [128, 8 * K], BF16, es_wp)
    WrT_b = sb("WrT_b", [128, D], BF16, es_wp)
    Wvg_b = sb("Wvg_b", [128, 8 * V], BF16, es_wp)
    C0 = sb("C0", [128, 4 * 512], BF16, es_wp)
    S0 = sb("S0", [128, 4 * 512], BF16, es_wp)

    es_w = contextlib.ExitStack()
    wstage = sb("wstage", [128, 4096], F32, es_w)
    wstage2 = sb("wstage2", [128, 4096], F32, es_w)
    wsmall = sb("wsmall", [128, 3 * 8 * K], F32, es_w)
    dq_s = sb("dq_s", [128, 96], F32, es_w)
    dq_r = sb("dq_r", [128, 96], F32, es_w)
    dq_t = sb("dq_t", [128, 3072], F32, es_w)
    dq_q = sb("dq_q", [128, 3072], BF16, es_w)
    stg_T = sb("stg_T", [128, 24 * 128], BF16, es_w)
    tb1 = sb("tb1", [128, 512], F32, es_w)
    tb2 = sb("tb2", [128, 512], F32, es_w)
    tb3 = sb("tb3", [128, 512], F32, es_w)
    tb4 = sb("tb4", [128, 512], F32, es_w)
    tmp_bf = sb("tmp_bf", [128, 128], BF16, es_w)

    for name, dst, src in [
        ("gln", c_gln, gln_i), ("xlu", c_xlu, xlu_i), ("xlv", c_xlv, xlv_i),
        ("om", c_om, omega_c), ("omr", c_omr, omega_r), ("idb", c_idb, ident_b),
        ("idf", c_idf, ident_f), ("mask", c_mask, maskb), ("t0", c_t0, t0s),
    ]:
        P.dma("wl1", (lambda e, d=dst, s=src: e.dma_start(out=d[:, :], in_=s)),
              reads=(), writes=("c_" + name,))
    P.op("gpsimd", lambda e: e.iota(c_ci[:, :], [[1, 512]], base=0,
                                    channel_multiplier=0,
                                    allow_small_or_imprecise_dtypes=True),
         writes=("c_ci",))

    def _ld_small(e):
        return e.dma_start(out=wsmall[:, 0:8 * K].rearrange("p (a k) -> p a k", a=8),
                           in_=Wq_i.rearrange("(a p) k -> p a k", p=128))
    P.dma("wl2", _ld_small, writes=("wsm_q",))
    P.dma("wl2", lambda e: e.dma_start(
        out=wsmall[:, 8 * K:16 * K].rearrange("p (a k) -> p a k", a=8),
        in_=Wk_i.rearrange("(a p) k -> p a k", p=128)), writes=("wsm_k",))
    P.dma("wl2", lambda e: e.dma_start(
        out=wsmall[:, 16 * K:24 * K].rearrange("p (a k) -> p a k", a=8),
        in_=Wr_i.rearrange("(a p) k -> p a k", p=128)), writes=("wsm_r",))
    for hc in range(8):
        P.op("vector", lambda e, hc=hc: e.tensor_scalar(
            out=Wq_b[:, hc * K:(hc + 1) * K], in0=wsmall[:, hc * K:(hc + 1) * K],
            scalar1=c_gln[:, hc:hc + 1], scalar2=TAU_INV, op0=ALU.mult, op1=ALU.mult),
            reads=("wsm_q", "c_gln"), writes=("Wq_b",))
        P.op("vector", lambda e, hc=hc: e.tensor_scalar(
            out=Wk_b[:, hc * K:(hc + 1) * K], in0=wsmall[:, (8 + hc) * K:(9 + hc) * K],
            scalar1=c_gln[:, hc:hc + 1], scalar2=None, op0=ALU.mult),
            reads=("wsm_k", "c_gln"), writes=("Wk_b",))
        P.op("vector", lambda e, hc=hc: e.tensor_copy(
            out=tmp_bf[:, :], in_=wsmall[:, (16 + hc) * K:(17 + hc) * K]),
            reads=("wsm_r",), writes=("tmp_bf",))
        P.op("tensor", lambda e, hc=hc: e.transpose(
            psb(5), tmp_bf[:, :], c_idb[:, :]),
            reads=("tmp_bf", "c_idb"), writes=("ps5",))
        P.op("vector", lambda e, hc=hc: e.tensor_copy(
            out=WrT_b[:, hc * 128:(hc + 1) * 128], in_=psb(5)),
            reads=("ps5",), writes=("WrT_b",))

    for hcg in range(4):
        buf = wstage if hcg % 2 == 0 else wstage2
        bkey = "wstg" if hcg % 2 == 0 else "wstg2"
        P.dma("wl1" if hcg % 2 == 0 else "wl2", (lambda e, hcg=hcg, buf=buf: e.dma_start(
            out=buf[:, :].rearrange("p (a v) -> p a v", a=2),
            in_=Wv_i[hcg * 256:(hcg + 1) * 256, :].rearrange("(a p) v -> p a v", p=128))),
            reads=(), writes=(bkey,))
        for j in range(2):
            hc = hcg * 2 + j
            P.op("vector", lambda e, hc=hc, j=j, buf=buf: e.tensor_scalar(
                out=Wvg_b[:, hc * V:(hc + 1) * V], in0=buf[:, j * V:(j + 1) * V],
                scalar1=c_gln[:, hc:hc + 1], scalar2=None, op0=ALU.mult),
                reads=(bkey, "c_gln"), writes=("Wvg_b",))

    def dequant(tile_key, tile_ap, width, ngr):
        P.op("vector", lambda e: e.tensor_reduce(
            out=dq_s[:, 0:ngr], in_=tile_ap.rearrange("p (g b) -> p g b", b=32),
            axis=AX.X, op=ALU.max, apply_absolute_value=True),
            reads=(tile_key,), writes=("dq_s",), w=ngr)
        P.op("vector", lambda e: e.tensor_scalar(
            out=dq_s[:, 0:ngr], in0=dq_s[:, 0:ngr], scalar1=float(1.0 / 7.0),
            scalar2=1e-12,
            op0=ALU.mult, op1=ALU.max), reads=("dq_s",), writes=("dq_s",), w=ngr)
        P.op("vector", lambda e: e.reciprocal(out=dq_r[:, 0:ngr], in_=dq_s[:, 0:ngr]),
             reads=("dq_s",), writes=("dq_r",), w=ngr)
        rb = bass.AP(dq_r.tensor if hasattr(dq_r, "tensor") else dq_r, 0,
                     [[96, 128], [1, ngr], [0, 32]]) if False else None
        def _mul_rinv(e):
            rview = dq_r[:, 0:ngr].unsqueeze(2).broadcast_to([128, ngr, 32])
            return e.tensor_tensor(out=dq_t[:, 0:width].rearrange("p (g b) -> p g b", b=32),
                                   in0=tile_ap.rearrange("p (g b) -> p g b", b=32),
                                   in1=rview, op=ALU.mult)
        P.op("vector", _mul_rinv, reads=(tile_key, "dq_r"), writes=("dq_t",))
        P.op("vector", lambda e: e.tensor_scalar(
            out=dq_t[:, 0:width], in0=dq_t[:, 0:width], scalar1=MAGIC, scalar2=MAGIC,
            op0=ALU.add, op1=ALU.subtract), reads=("dq_t",), writes=("dq_t",))
        P.op("vector", lambda e: e.tensor_scalar(
            out=dq_t[:, 0:width], in0=dq_t[:, 0:width], scalar1=-8.0, scalar2=7.0,
            op0=ALU.max, op1=ALU.min), reads=("dq_t",), writes=("dq_t",))
        def _mul_s(e):
            sview = dq_s[:, 0:ngr].unsqueeze(2).broadcast_to([128, ngr, 32])
            return e.tensor_tensor(out=dq_q[:, 0:width].rearrange("p (g b) -> p g b", b=32),
                                   in0=dq_t[:, 0:width].rearrange("p (g b) -> p g b", b=32),
                                   in1=sview, op=ALU.mult)
        P.op("vector", _mul_s, reads=("dq_t", "dq_s"), writes=("dq_q",))

    for w_in, w_scr, cname in [(wup_i, wupT_scr, "wuA"), (wgate_i, wgT_scr, "wuB")]:
        for icg in range(6):
            buf = wstage if icg % 2 == 0 else wstage2
            bkey = "wstg" if icg % 2 == 0 else "wstg2"
            P.dma("wl1" if icg % 2 == 0 else "wl2", (lambda e, w_in=w_in, icg=icg, buf=buf: e.dma_start(
                out=buf[:, :].rearrange("p (a h) -> p a h", a=4),
                in_=w_in[icg * 512:(icg + 1) * 512, :].rearrange("(a p) h -> p a h", p=128))),
                writes=(bkey,))
            for j in range(4):
                ic = icg * 4 + j
                dequant(bkey, buf[:, j * D:(j + 1) * D], D, 32)
                for hc in range(8):
                    pt = ps[5 + (hc % 2)]
                    pkey = "ps%d" % (5 + hc % 2)
                    pi5 = 5 + hc % 2
                    P.op("tensor", lambda e, hc=hc, pi5=pi5: e.transpose(
                        psb(pi5), dq_q[:, hc * 128:(hc + 1) * 128], c_idb[:, :]),
                        reads=("dq_q", "c_idb"), writes=(pkey,))
                    P.op("scalar", lambda e, hc=hc, pi5=pi5: e.copy(
                        out=stg_T[:, hc * 128:(hc + 1) * 128], in_=psb(pi5)),
                        reads=(pkey,), writes=("stg_T",))
                P.dma("wtS", (lambda e, ic=ic, w_scr=w_scr: e.dma_start(
                    out=w_scr[ic].rearrange("a p i -> p a i"),
                    in_=stg_T[:, 0:1024].rearrange("p (a i) -> p a i", a=8))),
                    reads=("stg_T",), writes=())
    for hc in range(8):
        buf = wstage if hc % 2 == 0 else wstage2
        bkey = "wstg" if hc % 2 == 0 else "wstg2"
        P.dma("wl1" if hc % 2 == 0 else "wl2", (lambda e, hc=hc, buf=buf: e.dma_start(
            out=buf[:, 0:3072], in_=wdown_i[hc * 128:(hc + 1) * 128, :])),
            writes=(bkey,))
        dequant(bkey, buf[:, 0:3072], 3072, 96)
        for ic in range(24):
            pt = ps[5 + (ic % 2)]
            pkey = "ps%d" % (5 + ic % 2)
            pi5 = 5 + ic % 2
            P.op("tensor", lambda e, ic=ic, pi5=pi5: e.transpose(
                psb(pi5), dq_q[:, ic * 128:(ic + 1) * 128], c_idb[:, :]),
                reads=("dq_q", "c_idb"), writes=(pkey,))
            P.op("scalar", lambda e, ic=ic, pi5=pi5: e.copy(
                out=stg_T[:, ic * 128:(ic + 1) * 128], in_=psb(pi5)),
                reads=(pkey,), writes=("stg_T",))
        def _st_wd(e, hc=hc):
            dst = wdT_scr[:, :, hc * 128:(hc + 1) * 128].rearrange("a p h -> p a h")
            return e.dma_start(out=dst,
                               in_=stg_T[:, 0:24 * 128].rearrange("p (a h) -> p a h", a=24))
        P.dma("wtS", _st_wd, reads=("stg_T",), writes=())

    def rangered_sin(src_key, src_ap, dst_key, dst_ap, nb2, nb3, width, shift):
        nb1 = tb4
        """dst = sin(src + shift), range-reduced. Uses tb buffers nb1 (angle+shift),
        nb2 (n), writes dst."""
        if shift != 0.0:
            P.op("vector", lambda e: e.tensor_scalar(
                out=nb1[:, 0:width], in0=src_ap, scalar1=shift, scalar2=None, op0=ALU.add),
                reads=(src_key,), writes=("tbA",), w=width)
            akey, aap = "tbA", nb1[:, 0:width]
        else:
            akey, aap = src_key, src_ap
        P.op("vector", lambda e, aap=aap: e.tensor_scalar(
            out=nb2[:, 0:width], in0=aap, scalar1=INV2PI, scalar2=MAGIC,
            op0=ALU.mult, op1=ALU.add), reads=(akey,), writes=("tbB",), w=width)
        P.op("vector", lambda e: e.tensor_scalar(
            out=nb2[:, 0:width], in0=nb2[:, 0:width], scalar1=MAGIC, scalar2=None,
            op0=ALU.subtract), reads=("tbB",), writes=("tbB",), w=width)
        P.op("vector", lambda e, aap=aap: e.scalar_tensor_tensor(
            out=nb3[:, 0:width], in0=nb2[:, 0:width], scalar=-C1, in1=aap,
            op0=ALU.mult, op1=ALU.add), reads=("tbB", akey), writes=("tbC",), w=width)
        P.op("vector", lambda e: e.scalar_tensor_tensor(
            out=nb3[:, 0:width], in0=nb2[:, 0:width], scalar=-C2, in1=nb3[:, 0:width],
            op0=ALU.mult, op1=ALU.add), reads=("tbB", "tbC"), writes=("tbC",), w=width)
        P.op("scalar", lambda e, dst_ap=dst_ap: e.activation(
            out=dst_ap, in_=nb3[:, 0:width], func=AF.Sin),
            reads=("tbC",), writes=(dst_key,), w=width)

    for mc in range(4):
        P.op("vector", lambda e, mc=mc: e.tensor_scalar(
            out=tb1[:, :], in0=c_ci[:, :], scalar1=c_om[:, mc:mc + 1], scalar2=None,
            op0=ALU.mult), reads=("c_ci", "c_om"), writes=("tb_ang",))
        rangered_sin("tb_ang", tb1[:, :], "S0", S0[:, mc * 512:(mc + 1) * 512],
                     tb2, tb3, 512, 0.0)
        rangered_sin("tb_ang", tb1[:, :], "C0", C0[:, mc * 512:(mc + 1) * 512],
                     tb2, tb3, 512, PIH)
    for mc in range(4):
        P.op("tensor", lambda e, mc=mc: e.matmul(
            out=ps[7][:, 0:NT0], lhsT=c_omr[0:1, mc * 128:(mc + 1) * 128],
            rhs=c_t0[0:1, :], start=True, stop=True),
            reads=("c_omr", "c_t0"), writes=("ps7",))
        P.op("vector", lambda e, mc=mc: e.tensor_copy(out=tb1[:, 0:NT0], in_=ps[7][:, 0:NT0]),
             reads=("ps7",), writes=("tb_ang",), w=NT0)
        rangered_sin("tb_ang", tb1[:, 0:NT0], "c_ts", c_ts[:, mc * NT0:(mc + 1) * NT0],
                     tb2, tb3, NT0, 0.0)
        rangered_sin("tb_ang", tb1[:, 0:NT0], "c_tc", c_tc[:, mc * NT0:(mc + 1) * NT0],
                     tb2, tb3, NT0, PIH)
    P.op("vector", lambda e: e.tensor_scalar(
        out=c_tsn[:, :], in0=c_ts[:, :], scalar1=-1.0, scalar2=None, op0=ALU.mult),
        reads=("c_ts",), writes=("c_tsn",), w=4 * NT0)

    P.end_phase()
    es_w.close()

    es_p = contextlib.ExitStack()
    xg = [sb("xg%d" % i, [128, 2 * D], F32, es_p) for i in range(2)]
    xT = [sb("xT%d" % i, [128, 8 * 512], BF16, es_p) for i in range(2)]
    xbf = sb("xbf", [128, D], BF16, es_p)
    rms_c = sb("rms_c", [128, 12], F32, es_p)
    sq_scr = sb("sq_scr", [128, D], F32, es_p)
    vst = [sb("vst%d" % i, [128, V], BF16, es_p) for i in range(2)]
    ct_sb = [sb("ct_sb%d" % i, [128, 4096], BF16, es_p) for i in range(2)]
    U_bf = sb("U_bf", [128, 8 * 256], BF16, es_p)
    qp_bf = sb("qp_bf", [128, 512], BF16, es_p)
    pp_bf = sb("pp_bf", [128, 512], BF16, es_p)
    qts = sb("qts", [128, 2304], BF16, es_p)
    gst = sb("gst", [128, V], BF16, es_p)
    ta = sb("ta", [128, 512], F32, es_p)
    tbb = sb("tbb", [128, 512], F32, es_p)
    tcq = sb("tcq", [128, 256], F32, es_p)
    tsq = sb("tsq", [128, 256], F32, es_p)

    def rms_xt(src_ap_fn, gi, ntile):
        xTb, xTk = xT[gi % 2], "xT%d" % (gi % 2)
        for half in range(ntile // 2):
            par = half % 2
            cnx = "xcA" if par == 0 else "xcB"
            P.dma(cnx, (lambda e, half=half, par=par: e.dma_start(
                out=xg[par][:, :].rearrange("p (t h) -> p t h", t=2),
                in_=src_ap_fn(half).rearrange("(t p) h -> p t h", p=128))),
                writes=("xg%d" % par,))
        for t in range(ntile):
            par = (t // 2) % 2
            xgb, xgk = xg[par], "xg%d" % par
            toff = (t % 2) * D
            P.op("scalar", lambda e, t=t, xgb=xgb, toff=toff: e.activation(
                out=sq_scr[:, :], in_=xgb[:, toff:toff + D], func=AF.Square,
                accum_out=rms_c[:, t:t + 1]), reads=(xgk,), writes=("sq_scr", "rms_c"))
            P.op("vector", lambda e, t=t: e.tensor_scalar(
                out=rms_c[:, 4 + t:5 + t], in0=rms_c[:, t:t + 1], scalar1=1.0 / D,
                scalar2=1e-6, op0=ALU.mult, op1=ALU.add),
                reads=("rms_c",), writes=("rms_c",), w=1)
            P.op("scalar", lambda e, t=t: e.activation(
                out=rms_c[:, 4 + t:5 + t], in_=rms_c[:, 4 + t:5 + t], func=AF.Sqrt),
                reads=("rms_c",), writes=("rms_c",))
            P.op("vector", lambda e, t=t: e.reciprocal(
                out=rms_c[:, 8 + t:9 + t], in_=rms_c[:, 4 + t:5 + t]),
                reads=("rms_c",), writes=("rms_c",), w=1)
            P.op("vector", lambda e, t=t, xgb=xgb, toff=toff: e.tensor_scalar(
                out=xbf[:, :], in0=xgb[:, toff:toff + D],
                scalar1=rms_c[:, 8 + t:9 + t], scalar2=None, op0=ALU.mult),
                reads=(xgk, "rms_c"), writes=("xbf",))
            for hc in range(8):
                pi5, pkey = 5 + hc % 2, "ps%d" % (5 + hc % 2)
                P.op("tensor", lambda e, hc=hc, pi5=pi5: e.transpose(
                    psb(pi5), xbf[:, hc * 128:(hc + 1) * 128], c_idb[:, :]),
                    reads=("xbf", "c_idb"), writes=(pkey,))
                P.op("scalar", lambda e, hc=hc, t=t, pi5=pi5, xTb=xTb: e.copy(
                    out=xTb[:, hc * 512 + t * 128: hc * 512 + (t + 1) * 128],
                    in_=psb(pi5)), reads=(pkey,), writes=(xTk,))
        return xTb, xTk

    for g in range(NG):
        xTb, xTk = rms_xt(
            lambda half, g=g: x_cov[g * KT + half * 256: g * KT + (half + 1) * 256, :],
            g, 4)
        for hc in range(8):
            P.op("tensor", lambda e, hc=hc, xTb=xTb: e.matmul(
                out=ps[4][:, :], lhsT=Wk_b[:, hc * K:(hc + 1) * K],
                rhs=xTb[:, hc * 512:(hc + 1) * 512], start=(hc == 0), stop=(hc == 7)),
                reads=("Wk_b", xTk), writes=("ps4",))
        P.op("vector", lambda e, g=g: e.tensor_copy(
            out=k_full[:, g * KT:(g + 1) * KT], in_=ps[4][:, :]),
            reads=("ps4",), writes=("k_full",))
        for t in range(4):
            for hc in range(8):
                for vg in range(4):
                    P.op("tensor", lambda e, hc=hc, vg=vg, t=t, xTb=xTb: e.matmul(
                        out=ps[vg][:, :],
                        lhsT=xTb[:, hc * 512 + t * 128: hc * 512 + (t + 1) * 128],
                        rhs=Wvg_b[:, hc * V + vg * 512: hc * V + (vg + 1) * 512],
                        start=(hc == 0), stop=(hc == 7)),
                        reads=(xTk, "Wvg_b"), writes=("ps%d" % vg,))
            vpar = t % 2
            for vg in range(4):
                P.op("scalar", lambda e, vg=vg, vpar=vpar: e.copy(
                    out=vst[vpar][:, vg * 512:(vg + 1) * 512], in_=ps[vg][:, :]),
                    reads=("ps%d" % vg,), writes=("vst%d" % vpar,))
            P.dma("vstA" if vpar == 0 else "vstB",
                  (lambda e, g=g, t=t, vpar=vpar: e.dma_start(
                      out=v_scr[g * 4 + t], in_=vst[vpar][:, :])),
                  reads=("vst%d" % vpar,), writes=())
        cpar = g % 2
        ckey = "ct_sb%d" % cpar
        for mc in range(4):
            ti = mc * NT0 + NP + g
            P.op("vector", lambda e, mc=mc, ti=ti: e.tensor_scalar(
                out=ta[:, :], in0=C0[:, mc * 512:(mc + 1) * 512],
                scalar1=c_tc[:, ti:ti + 1], scalar2=None, op0=ALU.mult),
                reads=("C0", "c_tc"), writes=("ta",))
            P.op("vector", lambda e, mc=mc, ti=ti, cpar=cpar: e.scalar_tensor_tensor(
                out=ct_sb[cpar][:, mc * 512:(mc + 1) * 512],
                in0=S0[:, mc * 512:(mc + 1) * 512], scalar=c_tsn[:, ti:ti + 1],
                in1=ta[:, :], op0=ALU.mult, op1=ALU.add),
                reads=("S0", "c_tsn", "ta"), writes=(ckey,))
            P.op("vector", lambda e, mc=mc, ti=ti: e.tensor_scalar(
                out=ta[:, :], in0=S0[:, mc * 512:(mc + 1) * 512],
                scalar1=c_tc[:, ti:ti + 1], scalar2=None, op0=ALU.mult),
                reads=("S0", "c_tc"), writes=("ta",))
            P.op("vector", lambda e, mc=mc, ti=ti, cpar=cpar: e.scalar_tensor_tensor(
                out=ct_sb[cpar][:, (4 + mc) * 512:(5 + mc) * 512],
                in0=C0[:, mc * 512:(mc + 1) * 512], scalar=c_ts[:, ti:ti + 1],
                in1=ta[:, :], op0=ALU.mult, op1=ALU.add),
                reads=("C0", "c_ts", "ta"), writes=(ckey,))
        P.dma("ctsA" if cpar == 0 else "ctsB",
              (lambda e, g=g, cpar=cpar: e.dma_start(
                  out=ct_scr[g], in_=ct_sb[cpar][:, :])),
              reads=(ckey,), writes=())

    P.end_phase()
    for hc in range(8):
        par = hc % 2
        P.dma("xcA" if par == 0 else "xcB", (lambda e, hc=hc, par=par: e.dma_start(
            out=xg[par][:, 0:V], in_=Wg_i[hc * 128:(hc + 1) * 128, :])),
            writes=("xg%d" % par,))
        P.op("vector", lambda e, hc=hc, par=par: e.tensor_scalar(
            out=Wvg_b[:, hc * V:(hc + 1) * V], in0=xg[par][:, 0:V],
            scalar1=c_gln[:, hc:hc + 1], scalar2=None, op0=ALU.mult),
            reads=("xg%d" % par, "c_gln"), writes=("Wvg_b",))

    for og in range(NOG):
        xTb, xTk = rms_xt(
            lambda half, og=og: x_own[og * KT + half * 256: og * KT + (half + 1) * 256, :],
            og, 4)
        for hc in range(8):
            P.op("tensor", lambda e, hc=hc, xTb=xTb: e.matmul(
                out=ps[4][:, :], lhsT=Wq_b[:, hc * K:(hc + 1) * K],
                rhs=xTb[:, hc * 512:(hc + 1) * 512], start=(hc == 0), stop=(hc == 7)),
                reads=("Wq_b", xTk), writes=("ps4",))
        P.op("vector", lambda e: e.tensor_scalar(
            out=qp_bf[:, :], in0=ps[4][:, :], scalar1=c_xlu[:, 0:1], scalar2=None,
            op0=ALU.add), reads=("ps4", "c_xlu"), writes=("qp_bf",))
        P.op("vector", lambda e: e.tensor_scalar(
            out=pp_bf[:, :], in0=ps[4][:, :], scalar1=c_xlv[:, 0:1], scalar2=None,
            op0=ALU.add), reads=("ps4", "c_xlv"), writes=("pp_bf",))
        for cc in range(2):
            p_idx = 2 * og + cc
            for dc in range(8):
                pt, pkey = ps[5 + dc % 2], "ps%d" % (5 + dc % 2)
                P.op("tensor", lambda e, dc=dc, cc=cc, pt=pt: e.matmul(
                    out=pt[:, 0:256], lhsT=WrT_b[:, dc * 128:(dc + 1) * 128],
                    rhs=pp_bf[:, cc * 256:(cc + 1) * 256], start=True, stop=True),
                    reads=("WrT_b", "pp_bf"), writes=(pkey,))
                P.op("vector", lambda e, dc=dc, pt=pt: e.tensor_copy(
                    out=U_bf[:, dc * 256:(dc + 1) * 256], in_=pt[:, 0:256]),
                    reads=(pkey,), writes=("U_bf",), w=256)
            P.op("vector", lambda e, cc=cc: e.tensor_copy(
                out=qts[:, 0:256],
                in_=qp_bf[:, cc * 256:(cc + 1) * 256]),
                reads=("qp_bf",), writes=("qts",), w=256)
            for mc in range(4):
                ti = mc * NT0 + p_idx
                P.op("vector", lambda e, mc=mc, ti=ti: e.tensor_scalar(
                    out=ta[:, 0:256], in0=C0[:, mc * 512:mc * 512 + 256],
                    scalar1=c_tc[:, ti:ti + 1], scalar2=None, op0=ALU.mult),
                    reads=("C0", "c_tc"), writes=("ta",), w=256)
                P.op("vector", lambda e, mc=mc, ti=ti: e.scalar_tensor_tensor(
                    out=tcq[:, :], in0=S0[:, mc * 512:mc * 512 + 256],
                    scalar=c_tsn[:, ti:ti + 1], in1=ta[:, 0:256],
                    op0=ALU.mult, op1=ALU.add),
                    reads=("S0", "c_tsn", "ta"), writes=("tcq",), w=256)
                P.op("vector", lambda e, mc=mc, ti=ti: e.tensor_scalar(
                    out=ta[:, 0:256], in0=S0[:, mc * 512:mc * 512 + 256],
                    scalar1=c_tc[:, ti:ti + 1], scalar2=None, op0=ALU.mult),
                    reads=("S0", "c_tc"), writes=("ta",), w=256)
                P.op("vector", lambda e, mc=mc, ti=ti: e.scalar_tensor_tensor(
                    out=tsq[:, :], in0=C0[:, mc * 512:mc * 512 + 256],
                    scalar=c_ts[:, ti:ti + 1], in1=ta[:, 0:256],
                    op0=ALU.mult, op1=ALU.add),
                    reads=("C0", "c_ts", "ta"), writes=("tsq",), w=256)
                us = U_bf[:, mc * 256:(mc + 1) * 256]
                uc = U_bf[:, (4 + mc) * 256:(5 + mc) * 256]
                P.op("vector", lambda e, us=us: e.tensor_tensor(
                    out=ta[:, 0:256], in0=us, in1=tsq[:, :], op=ALU.mult),
                    reads=("U_bf", "tsq"), writes=("ta",), w=256)
                P.op("vector", lambda e, uc=uc: e.tensor_tensor(
                    out=tbb[:, 0:256], in0=uc, in1=tcq[:, :], op=ALU.mult),
                    reads=("U_bf", "tcq"), writes=("tbb",), w=256)
                P.op("vector", lambda e, mc=mc: e.tensor_tensor(
                    out=qts[:, (1 + mc) * 256:(2 + mc) * 256],
                    in0=ta[:, 0:256], in1=tbb[:, 0:256], op=ALU.add),
                    reads=("ta", "tbb"), writes=("qts",), w=256)
                P.op("vector", lambda e, uc=uc: e.tensor_tensor(
                    out=ta[:, 0:256], in0=uc, in1=tsq[:, :], op=ALU.mult),
                    reads=("U_bf", "tsq"), writes=("ta",), w=256)
                P.op("vector", lambda e, us=us: e.tensor_tensor(
                    out=tbb[:, 0:256], in0=us, in1=tcq[:, :], op=ALU.mult),
                    reads=("U_bf", "tcq"), writes=("tbb",), w=256)
                P.op("vector", lambda e, mc=mc: e.tensor_tensor(
                    out=qts[:, (5 + mc) * 256:(6 + mc) * 256],
                    in0=ta[:, 0:256], in1=tbb[:, 0:256], op=ALU.subtract),
                    reads=("ta", "tbb"), writes=("qts",), w=256)
            P.dma("qtS", (lambda e, p_idx=p_idx: e.dma_start(
                out=qt_scr[p_idx], in_=qts[:, 0:2304])),
                reads=("qts",), writes=())
        for t in range(4):
            for hc in range(8):
                for vg in range(4):
                    P.op("tensor", lambda e, hc=hc, vg=vg, t=t, xTb=xTb: e.matmul(
                        out=ps[vg][:, :],
                        lhsT=xTb[:, hc * 512 + t * 128: hc * 512 + (t + 1) * 128],
                        rhs=Wvg_b[:, hc * V + vg * 512: hc * V + (vg + 1) * 512],
                        start=(hc == 0), stop=(hc == 7)),
                        reads=(xTk, "Wvg_b"), writes=("ps%d" % vg,))
            for vg in range(4):
                P.op("scalar", lambda e, vg=vg: e.activation(
                    out=gst[:, vg * 512:(vg + 1) * 512], in_=ps[vg][:, :], func=AF.Silu),
                    reads=("ps%d" % vg,), writes=("gst",))
            p_idx, sub = 2 * og + t // 2, t % 2
            P.dma("gS", (lambda e, p_idx=p_idx, sub=sub: e.dma_start(
                out=g_scr[p_idx, sub], in_=gst[:, :])),
                reads=("gst",), writes=())

    P.op("vector", lambda e: e.memset(amax_y1[:, :], 0.0), writes=("amax",), w=2)
    P.end_phase()
    es_p.close()
    es_wp.close()

    es_a = contextlib.ExitStack()
    Wo_b = sb("Wo_b", [128, 16 * D], BF16, es_a)
    strip = [sb("strip%d" % s, [128, SEGT * KT], F32, es_a) for s in range(2)]
    w_bf = [sb("w_bf%d" % s, [128, SEGT * KT], BF16, es_a) for s in range(2)]
    wTt = [sb("wT%d" % s, [128, SEGT * KT], BF16, es_a) for s in range(2)]
    wvg = [sb("wvg%d" % s, [128, V], BF16, es_a) for s in range(2)]
    wvgT = [sb("wvgT%d" % s, [128, V], BF16, es_a) for s in range(2)]
    wv_sb = [sb("wv_sb%d" % s, [128, V], F32, es_a) for s in range(2)]
    vb = [sb("vb%d" % i, [128, V], BF16, es_a) for i in range(4)]
    ctbuf = [sb("ctbuf%d" % i, [128, 4096], BF16, es_a) for i in range(2)]
    qtb = sb("qtb", [128, 2304], BF16, es_a)
    gbuf = sb("gbuf", [128, 2 * V], BF16, es_a)
    xob = sb("xob", [128, 2 * D], F32, es_a)
    y1st = [sb("y1st%d" % s, [128, D], F32, es_a) for s in range(2)]
    pen = sb("pen", [128, 4 * 512], F32, es_a)
    mlc = sb("mlc", [128, 16], F32, es_a)
    lnscr = sb("lnscr", [128, D], F32, es_a)
    lnt = sb("lnt", [128, D], F32, es_a)
    sqs2 = sb("sqs2", [128, D], F32, es_a)
    lncol = sb("lncol", [128, 8], F32, es_a)

    P.dma("woL", lambda e: e.dma_start(
        out=Wo_b[:, :].rearrange("p (a h) -> p a h", a=16),
        in_=Wo_i.rearrange("(a p) h -> p a h", p=128)), writes=("Wo_b",))
    P.op("vector", lambda e: e.memset(wv_sb[0][:, :], 0.0), writes=("wv_sb0",))
    P.op("vector", lambda e: e.memset(wv_sb[1][:, :], 0.0), writes=("wv_sb1",))

    for p in range(NP):
        ktn = 2 * (p + 1)
        P.dma("qtL", (lambda e, p=p: e.dma_start(out=qtb[:, :], in_=qt_scr[p])),
              writes=("qtb",))
        P.dma("gL", (lambda e, p=p: e.dma_start(
            out=gbuf[:, :].rearrange("p (s v) -> p s v", s=2),
            in_=g_scr[p].rearrange("s p v -> p s v"))), writes=("gbuf",))
        P.dma("xoL", (lambda e, p=p: e.dma_start(
            out=xob[:, :].rearrange("p (s h) -> p s h", s=2),
            in_=x_own[p * CH:(p + 1) * CH, :].rearrange("(s p) h -> p s h", p=128))),
            writes=("xob",))
        for sub in range(2):
            for w in range(2):
                sl = (sub * 2 + w)
                P.op("scalar", lambda e, p=p, sub=sub, w=w, sl=sl: e.activation(
                    out=pen[:, sl * 512:(sl + 1) * 512], in_=c_ci[:, :], func=AF.Identity,
                    bias=c_mask[:, p * 4 + w * 2 + sub: p * 4 + w * 2 + sub + 1], scale=1.0),
                    reads=("c_ci", "c_mask"), writes=("pen",))
                P.op("scalar", lambda e, sl=sl: e.sign(
                    out=pen[:, sl * 512:(sl + 1) * 512], in_=pen[:, sl * 512:(sl + 1) * 512]),
                    reads=("pen",), writes=("pen",))
                P.op("vector", lambda e, sl=sl: e.tensor_scalar(
                    out=pen[:, sl * 512:(sl + 1) * 512], in0=pen[:, sl * 512:(sl + 1) * 512],
                    scalar1=0.0, scalar2=MASK_PEN, op0=ALU.max, op1=ALU.mult),
                    reads=("pen",), writes=("pen",))
        P.op("vector", lambda e: e.memset(mlc[:, 0:2], NEG_BIG), writes=("mlc",), w=2)
        P.op("vector", lambda e: e.memset(mlc[:, 2:4], 0.0), writes=("mlc",), w=2)

        nseg = (ktn + SEGT - 1) // SEGT
        for s in range(nseg):
            tiles = list(range(s * SEGT, min((s + 1) * SEGT, ktn)))
            nst = len(tiles)
            for j, ktg in enumerate(tiles):
                cpar = ktg % 2
                P.dma("ctA" if cpar == 0 else "ctB", (lambda e, ktg=ktg, cpar=cpar: e.dma_start(
                    out=ctbuf[cpar][:, :], in_=ct_scr[ktg])),
                    writes=("ctbuf%d" % cpar,))
                for sub in range(2):
                    pk = "ps%d" % sub
                    for fc in range(9):
                        if fc == 0:
                            rhs, rkeys = k_full[:, ktg * KT:(ktg + 1) * KT], ("k_full",)
                        else:
                            rhs = ctbuf[cpar][:, (fc - 1) * 512: fc * 512]
                            rkeys = ("ctbuf%d" % cpar,)
                        P.op("tensor", lambda e, fc=fc, sub=sub, rhs=rhs: e.matmul(
                            out=ps[sub][:, :],
                            lhsT=qtb[:, fc * 256 + sub * 128: fc * 256 + (sub + 1) * 128],
                            rhs=rhs, start=(fc == 0), stop=(fc == 8)),
                            reads=("qtb",) + rkeys, writes=(pk,))
                    if ktg >= 2 * p:
                        sl = sub * 2 + (ktg - 2 * p)
                        P.op("vector", lambda e, j=j, sub=sub, sl=sl: e.tensor_tensor(
                            out=strip[sub][:, j * 512:(j + 1) * 512], in0=ps[sub][:, :],
                            in1=pen[:, sl * 512:(sl + 1) * 512], op=ALU.add),
                            reads=(pk, "pen"), writes=("strip%d" % sub,))
                    else:
                        P.op("vector", lambda e, j=j, sub=sub: e.tensor_copy(
                            out=strip[sub][:, j * 512:(j + 1) * 512], in_=ps[sub][:, :]),
                            reads=(pk,), writes=("strip%d" % sub,))
            segw = nst * 512
            for sub in range(2):
                sk = "strip%d" % sub
                P.op("vector", lambda e, sub=sub, segw=segw: e.reduce_max(
                    out=mlc[:, 4 + sub:5 + sub], in_=strip[sub][:, 0:segw], axis=AX.X,
                    negate=True), reads=(sk,), writes=("mlc",), w=1)
                P.op("vector", lambda e, sub=sub: e.tensor_scalar(
                    out=mlc[:, 6 + sub:7 + sub], in0=mlc[:, 4 + sub:5 + sub],
                    scalar1=-1.0, scalar2=None, op0=ALU.mult),
                    reads=("mlc",), writes=("mlc",), w=1)
                P.op("vector", lambda e, sub=sub: e.tensor_tensor(
                    out=mlc[:, 6 + sub:7 + sub], in0=mlc[:, 6 + sub:7 + sub],
                    in1=mlc[:, 0 + sub:1 + sub], op=ALU.max),
                    reads=("mlc",), writes=("mlc",), w=1)
                P.op("vector", lambda e, sub=sub: e.tensor_tensor(
                    out=mlc[:, 8 + sub:9 + sub], in0=mlc[:, 0 + sub:1 + sub],
                    in1=mlc[:, 6 + sub:7 + sub], op=ALU.subtract),
                    reads=("mlc",), writes=("mlc",), w=1)
                P.op("scalar", lambda e, sub=sub: e.activation(
                    out=mlc[:, 8 + sub:9 + sub], in_=mlc[:, 8 + sub:9 + sub], func=AF.Exp),
                    reads=("mlc",), writes=("mlc",), w=1)
                P.op("vector", lambda e, sub=sub: e.tensor_copy(
                    out=mlc[:, 0 + sub:1 + sub], in_=mlc[:, 6 + sub:7 + sub]),
                    reads=("mlc",), writes=("mlc",), w=1)
                P.op("vector", lambda e, sub=sub: e.tensor_scalar(
                    out=mlc[:, 14:15], in0=mlc[:, 6 + sub:7 + sub], scalar1=-1.0,
                    scalar2=None, op0=ALU.mult), reads=("mlc",), writes=("mlc",), w=1)
                P.op("scalar", lambda e, sub=sub, segw=segw: e.activation(
                    out=w_bf[sub][:, 0:segw], in_=strip[sub][:, 0:segw], func=AF.Exp,
                    bias=mlc[:, 14:15], scale=1.0, accum_out=mlc[:, 10 + sub:11 + sub]),
                    reads=(sk, "mlc"), writes=("w_bf%d" % sub, "mlc"), force_inc=True)
                P.op("vector", lambda e, sub=sub: e.scalar_tensor_tensor(
                    out=mlc[:, 2 + sub:3 + sub], in0=mlc[:, 2 + sub:3 + sub],
                    scalar=mlc[:, 8 + sub:9 + sub], in1=mlc[:, 10 + sub:11 + sub],
                    op0=ALU.mult, op1=ALU.add), reads=("mlc",), writes=("mlc",), w=1)
                for u in range(4 * nst):
                    pi5, pkey = 2 + u % 2, "ps%d" % (2 + u % 2)
                    P.op("tensor", lambda e, u=u, sub=sub, pi5=pi5: e.transpose(
                        psb(pi5), w_bf[sub][:, u * 128:(u + 1) * 128], c_idb[:, :]),
                        reads=("w_bf%d" % sub, "c_idb"), writes=(pkey,))
                    P.op("scalar", lambda e, u=u, sub=sub, pi5=pi5: e.copy(
                        out=wTt[sub][:, u * 128:(u + 1) * 128], in_=psb(pi5)),
                        reads=(pkey,), writes=("wT%d" % sub,))
            for j, ktg in enumerate(tiles):
                for u in range(4):
                    bi = (j * 4 + u) % 4
                    P.dma("vA" if u % 2 == 0 else "vB",
                          (lambda e, ktg=ktg, u=u, bi=bi: e.dma_start(
                              out=vb[bi][:, :], in_=v_scr[ktg * 4 + u])),
                          writes=("vb%d" % bi,))
                for u in range(4):
                    bi = (j * 4 + u) % 4
                    for sub in range(2):
                        for vg in range(4):
                            P.op("tensor", lambda e, j=j, u=u, vg=vg, sub=sub, bi=bi: e.matmul(
                                out=ps[sub * 4 + vg][:, :],
                                lhsT=wTt[sub][:, (j * 4 + u) * 128:(j * 4 + u + 1) * 128],
                                rhs=vb[bi][:, vg * 512:(vg + 1) * 512],
                                start=(j == 0 and u == 0), stop=(j == nst - 1 and u == 3)),
                                reads=("wT%d" % sub, "vb%d" % bi),
                                writes=("ps%d" % (sub * 4 + vg),))
            for sub in range(2):
                for vg in range(4):
                    P.op("vector", lambda e, vg=vg, sub=sub: e.scalar_tensor_tensor(
                        out=wv_sb[sub][:, vg * 512:(vg + 1) * 512],
                        in0=wv_sb[sub][:, vg * 512:(vg + 1) * 512],
                        scalar=mlc[:, 8 + sub:9 + sub], in1=ps[sub * 4 + vg][:, :],
                        op0=ALU.mult, op1=ALU.add),
                        reads=("wv_sb%d" % sub, "mlc", "ps%d" % (sub * 4 + vg)),
                        writes=("wv_sb%d" % sub,))
        for sub in range(2):
            P.op("vector", lambda e, sub=sub: e.reciprocal(
                out=mlc[:, 12 + sub:13 + sub], in_=mlc[:, 2 + sub:3 + sub]),
                reads=("mlc",), writes=("mlc",), w=1)
            P.op("vector", lambda e, sub=sub: e.scalar_tensor_tensor(
                out=wvg[sub][:, :], in0=wv_sb[sub][:, :], scalar=mlc[:, 12 + sub:13 + sub],
                in1=gbuf[:, sub * V:(sub + 1) * V], op0=ALU.mult, op1=ALU.mult),
                reads=("wv_sb%d" % sub, "mlc", "gbuf"), writes=("wvg%d" % sub,))
            for vc in range(16):
                pi5, pkey = 2 + vc % 2, "ps%d" % (2 + vc % 2)
                P.op("tensor", lambda e, vc=vc, sub=sub, pi5=pi5: e.transpose(
                    psb(pi5), wvg[sub][:, vc * 128:(vc + 1) * 128], c_idb[:, :]),
                    reads=("wvg%d" % sub, "c_idb"), writes=(pkey,))
                P.op("scalar", lambda e, vc=vc, sub=sub, pi5=pi5: e.copy(
                    out=wvgT[sub][:, vc * 128:(vc + 1) * 128], in_=psb(pi5)),
                    reads=(pkey,), writes=("wvgT%d" % sub,))
            pa, pb = (0, 1) if sub == 0 else (4, 5)
            for vc in range(16):
                P.op("tensor", lambda e, vc=vc, sub=sub, pa=pa: e.matmul(
                    out=ps[pa][:, :], lhsT=wvgT[sub][:, vc * 128:(vc + 1) * 128],
                    rhs=Wo_b[:, vc * D: vc * D + 512], start=(vc == 0), stop=(vc == 15)),
                    reads=("wvgT%d" % sub, "Wo_b"), writes=("ps%d" % pa,))
                P.op("tensor", lambda e, vc=vc, sub=sub, pb=pb: e.matmul(
                    out=ps[pb][:, :], lhsT=wvgT[sub][:, vc * 128:(vc + 1) * 128],
                    rhs=Wo_b[:, vc * D + 512: (vc + 1) * D], start=(vc == 0), stop=(vc == 15)),
                    reads=("wvgT%d" % sub, "Wo_b"), writes=("ps%d" % pb,))
            P.op("vector", lambda e, sub=sub, pa=pa: e.tensor_tensor(
                out=lnt[:, 0:512], in0=ps[pa][:, :], in1=xob[:, sub * D: sub * D + 512],
                op=ALU.add), reads=("ps%d" % pa, "xob"), writes=("lnt",))
            P.op("vector", lambda e, sub=sub, pb=pb: e.tensor_tensor(
                out=lnt[:, 512:1024], in0=ps[pb][:, :],
                in1=xob[:, sub * D + 512:(sub + 1) * D], op=ALU.add),
                reads=("ps%d" % pb, "xob"), writes=("lnt",))
            P.op("vector", lambda e: e.reduce_sum(
                out=lncol[:, 0:1], in_=lnt[:, :], axis=AX.X), reads=("lnt",), writes=("lncol",), w=1)
            P.op("vector", lambda e: e.tensor_scalar(
                out=lncol[:, 1:2], in0=lncol[:, 0:1], scalar1=1.0 / D, scalar2=None,
                op0=ALU.mult), reads=("lncol",), writes=("lncol",), w=1)
            P.op("vector", lambda e: e.tensor_scalar(
                out=lnscr[:, :], in0=lnt[:, :], scalar1=lncol[:, 1:2], scalar2=None,
                op0=ALU.subtract), reads=("lnt", "lncol"), writes=("lnscr",))
            P.op("scalar", lambda e: e.activation(
                out=sqs2[:, :], in_=lnscr[:, :], func=AF.Square,
                accum_out=lncol[:, 2:3]), reads=("lnscr",), writes=("sqs2", "lncol"))
            P.op("vector", lambda e: e.tensor_scalar(
                out=lncol[:, 4:5], in0=lncol[:, 2:3], scalar1=1.0 / D, scalar2=1e-5,
                op0=ALU.mult, op1=ALU.add), reads=("lncol",), writes=("lncol",), w=1)
            P.op("scalar", lambda e: e.activation(
                out=lncol[:, 4:5], in_=lncol[:, 4:5], func=AF.Sqrt),
                reads=("lncol",), writes=("lncol",), w=1)
            P.op("vector", lambda e: e.reciprocal(
                out=lncol[:, 3:4], in_=lncol[:, 4:5]), reads=("lncol",), writes=("lncol",), w=1)
            P.op("vector", lambda e, sub=sub: e.tensor_scalar(
                out=y1st[sub][:, :], in0=lnscr[:, :], scalar1=lncol[:, 3:4], scalar2=None,
                op0=ALU.mult), reads=("lnscr", "lncol"), writes=("y1st%d" % sub,))
            P.op("vector", lambda e, sub=sub: e.tensor_reduce(
                out=amax_y1[:, 1:2], in_=y1st[sub][:, :], axis=AX.X, op=ALU.max,
                apply_absolute_value=True), reads=("y1st%d" % sub,), writes=("amax_t",), w=1)
            P.op("vector", lambda e: e.tensor_tensor(
                out=amax_y1[:, 0:1], in0=amax_y1[:, 0:1], in1=amax_y1[:, 1:2], op=ALU.max),
                reads=("amax_t", "amax"), writes=("amax",), w=1)
            P.dma("y1S", (lambda e, p=p, sub=sub: e.dma_start(
                out=y1_scr[p, sub], in_=y1st[sub][:, :])),
                reads=("y1st%d" % sub,), writes=())

    def ar_pre(slot, src_col_ap, src_keys):
        P.op("tensor", lambda e, src_col_ap=src_col_ap: e.matmul(
            out=ps[6][0:1, 0:128], lhsT=src_col_ap, rhs=c_idf[:, :], start=True, stop=True),
            reads=src_keys + ("c_idf",), writes=("ps6",))
        P.op("vector", lambda e, slot=slot: e.reduce_max(
            out=gmx[0:1, slot:slot + 1], in_=ps[6][0:1, 0:128], axis=AX.X),
            reads=("ps6",), writes=("gmx",), w=1)
        P.dma("ccB", (lambda e, slot=slot: e.dma_start(
            out=cc_ins[slot], in_=gmx[0:1, slot:slot + 1])),
            reads=("gmx",), writes=("cc_in",))
        P.op_inc("gpsimd", lambda e, slot=slot: e.collective_compute(
            "AllReduce", ALU.max, replica_groups=[[0, 1, 2, 3, 4, 5, 6, 7]],
            ins=[cc_ins[slot]], outs=[cc_outs[slot]]),
            ccsem, 1, reads=("cc_in",), writes=("cc_out",))

    ar_pre(0, amax_y1[:, 0:1], ("amax",))
    P.end_phase()
    nc.all_core_barrier()
    es_a.close()

    es_f = contextlib.ExitStack()
    NTT = OWN // 128
    gate_sb = sb("gate_sb", [128, 24 * OWN], BF16, es_f)
    scc = sb("scc", [128, 16], F32, es_f)
    umg = sb("umg", [128, 4], F32, es_f)
    y1b = [sb("y1b%d" % i, [128, D], F32, es_f) for i in range(2)]
    ocol = sb("ocol", [128, 6], F32, es_f)
    f1 = sb("f1", [128, max(OWN, D)], F32, es_f)
    f2 = sb("f2", [128, max(OWN, D)], F32, es_f)
    f3 = sb("f3", [128, max(OWN, D)], F32, es_f)
    es_up = contextlib.ExitStack()
    wub = [sb("wub%d" % i, [128, 8 * 128], BF16, es_up) for i in range(2)]
    wgb = [sb("wgb%d" % i, [128, 8 * 128], BF16, es_up) for i in range(2)]
    upst = sb("upst", [128, OWN], BF16, es_up)
    upb = [sb("upb%d" % i, [128, OWN], BF16, es_up) for i in range(2)]
    es_tT = contextlib.ExitStack()
    tT = sb("tT", [128, 8 * OWN], BF16, es_tT)
    tqf = f1
    tbf16 = f2[:, :].bitcast(BF16)[:, 0:D]

    def ar_post(slot, inv_idx):
        P.dma("ccB", (lambda e, slot=slot: e.dma_start(
            out=scc[:, 15:16], in_=cc_outs[slot].partition_broadcast(128))),
            reads=("cc_out",), writes=("scc",))
        P.op("vector", lambda e, inv_idx=inv_idx: e.tensor_scalar(
            out=scc[:, inv_idx:inv_idx + 1], in0=scc[:, 15:16],
            scalar1=float(1.0 / 32767.0),
            scalar2=1e-12, op0=ALU.mult, op1=ALU.max), reads=("scc",), writes=("scc",), w=1)
        P.op("vector", lambda e, inv_idx=inv_idx: e.reciprocal(
            out=scc[:, inv_idx + 1:inv_idx + 2], in_=scc[:, inv_idx:inv_idx + 1]),
            reads=("scc",), writes=("scc",), w=1)

    ar_post(0, 0)

    for tt in range(NTT):
        par = tt % 2
        cn = "y1LA" if par == 0 else "y1LB"
        pi, si = tt // 2, tt % 2
        P.dma(cn, (lambda e, pi=pi, si=si, par=par: e.dma_start(
            out=y1b[par][:, :], in_=y1_scr[pi, si])), writes=("y1b%d" % par,))
        P.op("vector", lambda e, par=par: e.tensor_scalar(
            out=tqf[:, 0:D], in0=y1b[par][:, :], scalar1=scc[:, 1:2], scalar2=MAGIC,
            op0=ALU.mult, op1=ALU.add), reads=("y1b%d" % par, "scc"), writes=("f1",))
        P.op("vector", lambda e: e.tensor_scalar(
            out=tqf[:, 0:D], in0=tqf[:, 0:D], scalar1=MAGIC, scalar2=-32768.0,
            op0=ALU.subtract, op1=ALU.max), reads=("f1",), writes=("f1",))
        P.op("vector", lambda e: e.tensor_scalar(
            out=tbf16, in0=tqf[:, 0:D], scalar1=32767.0, scalar2=scc[:, 0:1],
            op0=ALU.min, op1=ALU.mult), reads=("f1", "scc"), writes=("f2",))
        for hc in range(8):
            pi5, pkey = hc % 2, "ps%d" % (hc % 2)
            P.op("tensor", lambda e, hc=hc, pi5=pi5: e.transpose(
                psb(pi5), tbf16[:, hc * 128:(hc + 1) * 128], c_idb[:, :]),
                reads=("f2", "c_idb"), writes=(pkey,))
            P.op("scalar", lambda e, hc=hc, tt=tt, pi5=pi5: e.copy(
                out=tT[:, hc * OWN + tt * 128: hc * OWN + (tt + 1) * 128],
                in_=psb(pi5)), reads=(pkey,), writes=("tT",))

    P.op("vector", lambda e: e.memset(umg[:, :], 0.0), writes=("umg",), w=4)
    NT4 = OWN // 512
    for ic in range(24):
        par = ic % 2
        P.dma("wuA" if par == 0 else "wuB", (lambda e, ic=ic, par=par: e.dma_start(
            out=wub[par][:, :].rearrange("p (a i) -> p a i", a=8),
            in_=wupT_scr[ic].rearrange("a p i -> p a i"))), writes=("wub%d" % par,))
        P.dma("wgA" if par == 0 else "wgB", (lambda e, ic=ic, par=par: e.dma_start(
            out=wgb[par][:, :].rearrange("p (a i) -> p a i", a=8),
            in_=wgT_scr[ic].rearrange("a p i -> p a i"))), writes=("wgb%d" % par,))
        for t4 in range(NT4):
            for hc in range(8):
                P.op("tensor", lambda e, hc=hc, t4=t4, par=par: e.matmul(
                    out=ps[t4][:, :], lhsT=wub[par][:, hc * 128:(hc + 1) * 128],
                    rhs=tT[:, hc * OWN + t4 * 512: hc * OWN + (t4 + 1) * 512],
                    start=(hc == 0), stop=(hc == 7)),
                    reads=("wub%d" % par, "tT"), writes=("ps%d" % t4,))
        for t4 in range(NT4):
            P.op("scalar", lambda e, t4=t4: e.copy(
                out=upst[:, t4 * 512:(t4 + 1) * 512], in_=ps[t4][:, :]),
                reads=("ps%d" % t4,), writes=("upst",))
        P.dma("upS", (lambda e, ic=ic: e.dma_start(out=up_scr[ic], in_=upst[:, :])),
              reads=("upst",), writes=())
        P.op("vector", lambda e: e.tensor_reduce(
            out=ocol[:, 0:1], in_=upst[:, :], axis=AX.X, op=ALU.max,
            apply_absolute_value=True), reads=("upst",), writes=("ocol",), w=1)
        P.op("vector", lambda e: e.tensor_tensor(
            out=umg[:, 0:1], in0=umg[:, 0:1], in1=ocol[:, 0:1], op=ALU.max),
            reads=("ocol", "umg"), writes=("umg",), w=1)
        for t4 in range(NT4):
            for hc in range(8):
                P.op("tensor", lambda e, hc=hc, t4=t4, par=par: e.matmul(
                    out=ps[4 + t4][:, :], lhsT=wgb[par][:, hc * 128:(hc + 1) * 128],
                    rhs=tT[:, hc * OWN + t4 * 512: hc * OWN + (t4 + 1) * 512],
                    start=(hc == 0), stop=(hc == 7)),
                    reads=("wgb%d" % par, "tT"), writes=("ps%d" % (4 + t4),))
        for t4 in range(NT4):
            P.op("scalar", lambda e, t4=t4, ic=ic: e.copy(
                out=gate_sb[:, ic * OWN + t4 * 512: ic * OWN + (t4 + 1) * 512],
                in_=ps[4 + t4][:, :]), reads=("ps%d" % (4 + t4),), writes=("gate_sb",))
        P.op("vector", lambda e, ic=ic: e.tensor_reduce(
            out=ocol[:, 1:2], in_=gate_sb[:, ic * OWN:(ic + 1) * OWN], axis=AX.X,
            op=ALU.max, apply_absolute_value=True), reads=("gate_sb",), writes=("ocol",), w=1)
        P.op("vector", lambda e: e.tensor_tensor(
            out=umg[:, 1:2], in0=umg[:, 1:2], in1=ocol[:, 1:2], op=ALU.max),
            reads=("ocol", "umg"), writes=("umg",), w=1)

    es_tT.close()
    ar_pre(1, umg[:, 0:1], ("umg",))
    ar_pre(2, umg[:, 1:2], ("umg",))
    P.end_phase()
    nc.all_core_barrier()
    ar_post(1, 2)
    ar_post(2, 4)

    for ic in range(24):
        gsl = gate_sb[:, ic * OWN:(ic + 1) * OWN]
        P.op("vector", lambda e, gsl=gsl: e.tensor_scalar(
            out=f1[:, 0:OWN], in0=gsl, scalar1=scc[:, 5:6], scalar2=MAGIC,
            op0=ALU.mult, op1=ALU.add), reads=("gate_sb", "scc"), writes=("f1",))
        P.op("vector", lambda e: e.tensor_scalar(
            out=f1[:, 0:OWN], in0=f1[:, 0:OWN], scalar1=MAGIC, scalar2=-32768.0,
            op0=ALU.subtract, op1=ALU.max), reads=("f1",), writes=("f1",))
        P.op("vector", lambda e: e.tensor_scalar(
            out=f1[:, 0:OWN], in0=f1[:, 0:OWN], scalar1=32767.0, scalar2=scc[:, 4:5],
            op0=ALU.min, op1=ALU.mult), reads=("f1", "scc"), writes=("f1",))
        P.op("scalar", lambda e: e.activation(
            out=f2[:, 0:OWN], in_=f1[:, 0:OWN], func=AF.Sigmoid), reads=("f1",), writes=("f2",))
        P.op("vector", lambda e: e.tensor_scalar(
            out=f2[:, 0:OWN], in0=f2[:, 0:OWN], scalar1=65536.0, scalar2=MAGIC,
            op0=ALU.mult, op1=ALU.add), reads=("f2",), writes=("f2",))
        P.op("vector", lambda e: e.tensor_scalar(
            out=f2[:, 0:OWN], in0=f2[:, 0:OWN], scalar1=MAGIC, scalar2=0.0,
            op0=ALU.subtract, op1=ALU.max), reads=("f2",), writes=("f2",))
        P.op("vector", lambda e: e.tensor_scalar(
            out=f2[:, 0:OWN], in0=f2[:, 0:OWN], scalar1=65535.0, scalar2=1.0 / 65536.0,
            op0=ALU.min, op1=ALU.mult), reads=("f2",), writes=("f2",))
        P.op("vector", lambda e: e.tensor_tensor(
            out=f3[:, 0:OWN], in0=f1[:, 0:OWN], in1=f2[:, 0:OWN], op=ALU.mult),
            reads=("f1", "f2"), writes=("f3",))
        P.op("vector", lambda e: e.tensor_reduce(
            out=ocol[:, 2:3], in_=f3[:, 0:OWN], axis=AX.X, op=ALU.max,
            apply_absolute_value=True), reads=("f3",), writes=("ocol",), w=1)
        P.op("vector", lambda e: e.tensor_tensor(
            out=umg[:, 2:3], in0=umg[:, 2:3], in1=ocol[:, 2:3], op=ALU.max),
            reads=("ocol", "umg"), writes=("umg",), w=1)
        P.op("vector", lambda e, gsl=gsl: e.tensor_copy(out=gsl, in_=f3[:, 0:OWN]),
             reads=("f3",), writes=("gate_sb",))

    ar_pre(3, umg[:, 2:3], ("umg",))
    P.end_phase()
    nc.all_core_barrier()
    ar_post(3, 6)

    P.op("vector", lambda e: e.memset(umg[:, 3:4], 0.0), writes=("umg",), w=1)
    for ic in range(24):
        par = ic % 2
        P.dma("upLA" if par == 0 else "upLB", (lambda e, ic=ic, par=par: e.dma_start(
            out=upb[par][:, :], in_=up_scr[ic])), writes=("upb%d" % par,))
        gsl = gate_sb[:, ic * OWN:(ic + 1) * OWN]
        P.op("vector", lambda e, par=par: e.tensor_scalar(
            out=f1[:, 0:OWN], in0=upb[par][:, :], scalar1=scc[:, 3:4], scalar2=MAGIC,
            op0=ALU.mult, op1=ALU.add), reads=("upb%d" % par, "scc"), writes=("f1",))
        P.op("vector", lambda e: e.tensor_scalar(
            out=f1[:, 0:OWN], in0=f1[:, 0:OWN], scalar1=MAGIC, scalar2=-32768.0,
            op0=ALU.subtract, op1=ALU.max), reads=("f1",), writes=("f1",))
        P.op("vector", lambda e: e.tensor_scalar(
            out=f1[:, 0:OWN], in0=f1[:, 0:OWN], scalar1=32767.0, scalar2=scc[:, 2:3],
            op0=ALU.min, op1=ALU.mult), reads=("f1", "scc"), writes=("f1",))
        P.op("vector", lambda e, gsl=gsl: e.tensor_scalar(
            out=f2[:, 0:OWN], in0=gsl, scalar1=scc[:, 7:8], scalar2=MAGIC,
            op0=ALU.mult, op1=ALU.add), reads=("gate_sb", "scc"), writes=("f2",))
        P.op("vector", lambda e: e.tensor_scalar(
            out=f2[:, 0:OWN], in0=f2[:, 0:OWN], scalar1=MAGIC, scalar2=-32768.0,
            op0=ALU.subtract, op1=ALU.max), reads=("f2",), writes=("f2",))
        P.op("vector", lambda e: e.tensor_scalar(
            out=f2[:, 0:OWN], in0=f2[:, 0:OWN], scalar1=32767.0, scalar2=scc[:, 6:7],
            op0=ALU.min, op1=ALU.mult), reads=("f2", "scc"), writes=("f2",))
        P.op("vector", lambda e: e.tensor_tensor(
            out=f3[:, 0:OWN], in0=f1[:, 0:OWN], in1=f2[:, 0:OWN], op=ALU.mult),
            reads=("f1", "f2"), writes=("f3",))
        P.op("vector", lambda e: e.tensor_reduce(
            out=ocol[:, 3:4], in_=f3[:, 0:OWN], axis=AX.X, op=ALU.max,
            apply_absolute_value=True), reads=("f3",), writes=("ocol",), w=1)
        P.op("vector", lambda e: e.tensor_tensor(
            out=umg[:, 3:4], in0=umg[:, 3:4], in1=ocol[:, 3:4], op=ALU.max),
            reads=("ocol", "umg"), writes=("umg",), w=1)
        P.op("vector", lambda e, gsl=gsl: e.tensor_copy(out=gsl, in_=f3[:, 0:OWN]),
             reads=("f3",), writes=("gate_sb",))

    ar_pre(4, umg[:, 3:4], ("umg",))
    P.end_phase()
    nc.all_core_barrier()
    ar_post(4, 8)

    for ic in range(24):
        gsl = gate_sb[:, ic * OWN:(ic + 1) * OWN]
        P.op("vector", lambda e, gsl=gsl: e.tensor_scalar(
            out=f1[:, 0:OWN], in0=gsl, scalar1=scc[:, 9:10], scalar2=MAGIC,
            op0=ALU.mult, op1=ALU.add), reads=("gate_sb", "scc"), writes=("f1",))
        P.op("vector", lambda e: e.tensor_scalar(
            out=f1[:, 0:OWN], in0=f1[:, 0:OWN], scalar1=MAGIC, scalar2=-32768.0,
            op0=ALU.subtract, op1=ALU.max), reads=("f1",), writes=("f1",))
        P.op("vector", lambda e, gsl=gsl: e.tensor_scalar(
            out=gsl, in0=f1[:, 0:OWN], scalar1=32767.0, scalar2=scc[:, 8:9],
            op0=ALU.min, op1=ALU.mult), reads=("f1", "scc"), writes=("gate_sb",))

    es_up.close()
    wdT = sb("wdT", [128, 24 * D], BF16, es_f)
    P.dma("wdl", lambda e: e.dma_start(
        out=wdT[:, :].rearrange("p (a h) -> p a h", a=24),
        in_=wdT_scr.rearrange("a p h -> p a h")), writes=("wdT",))
    for tt in range(NTT):
        par = tt % 2
        pa, pb = (0, 1) if par == 0 else (2, 3)
        for ic in range(24):
            P.op("tensor", lambda e, ic=ic, tt=tt, pa=pa: e.matmul(
                out=ps[pa][:, :], lhsT=gate_sb[:, ic * OWN + tt * 128: ic * OWN + (tt + 1) * 128],
                rhs=wdT[:, ic * D: ic * D + 512], start=(ic == 0), stop=(ic == 23)),
                reads=("gate_sb", "wdT"), writes=("ps%d" % pa,))
            P.op("tensor", lambda e, ic=ic, tt=tt, pb=pb: e.matmul(
                out=ps[pb][:, :], lhsT=gate_sb[:, ic * OWN + tt * 128: ic * OWN + (tt + 1) * 128],
                rhs=wdT[:, ic * D + 512:(ic + 1) * D], start=(ic == 0), stop=(ic == 23)),
                reads=("gate_sb", "wdT"), writes=("ps%d" % pb,))
        pi, si = tt // 2, tt % 2
        cn = "y1LA" if par == 0 else "y1LB"
        P.dma(cn, (lambda e, pi=pi, si=si, par=par: e.dma_start(
            out=y1b[par][:, :], in_=y1_scr[pi, si])), writes=("y1b%d" % par,))
        P.op("vector", lambda e, pa=pa, par=par: e.tensor_tensor(
            out=f1[:, 0:512], in0=ps[pa][:, :], in1=y1b[par][:, 0:512], op=ALU.add),
            reads=("ps%d" % pa, "y1b%d" % par), writes=("f1",))
        P.op("vector", lambda e, pb=pb, par=par: e.tensor_tensor(
            out=f1[:, 512:1024], in0=ps[pb][:, :], in1=y1b[par][:, 512:1024], op=ALU.add),
            reads=("ps%d" % pb, "y1b%d" % par), writes=("f1",))
        P.op("vector", lambda e: e.reduce_sum(
            out=ocol[:, 0:1], in_=f1[:, 0:D], axis=AX.X), reads=("f1",), writes=("ocol",), w=1)
        P.op("vector", lambda e: e.tensor_scalar(
            out=ocol[:, 1:2], in0=ocol[:, 0:1], scalar1=1.0 / D, scalar2=None,
            op0=ALU.mult), reads=("ocol",), writes=("ocol",), w=1)
        P.op("vector", lambda e: e.tensor_scalar(
            out=f2[:, 0:D], in0=f1[:, 0:D], scalar1=ocol[:, 1:2], scalar2=None,
            op0=ALU.subtract), reads=("f1", "ocol"), writes=("f2",))
        P.op("scalar", lambda e: e.activation(
            out=f3[:, 0:D], in_=f2[:, 0:D], func=AF.Square, accum_out=ocol[:, 2:3]),
            reads=("f2",), writes=("f3", "ocol"))
        P.op("vector", lambda e: e.tensor_scalar(
            out=ocol[:, 4:5], in0=ocol[:, 2:3], scalar1=1.0 / D, scalar2=1e-5,
            op0=ALU.mult, op1=ALU.add), reads=("ocol",), writes=("ocol",), w=1)
        P.op("scalar", lambda e: e.activation(
            out=ocol[:, 4:5], in_=ocol[:, 4:5], func=AF.Sqrt),
            reads=("ocol",), writes=("ocol",), w=1)
        P.op("vector", lambda e: e.reciprocal(
            out=ocol[:, 3:4], in_=ocol[:, 4:5]), reads=("ocol",), writes=("ocol",), w=1)
        P.op("vector", lambda e: e.tensor_scalar(
            out=f3[:, 0:D], in0=f2[:, 0:D], scalar1=ocol[:, 3:4], scalar2=None,
            op0=ALU.mult), reads=("f2", "ocol"), writes=("f3",))
        P.dma("outS", (lambda e, tt=tt: e.dma_start(
            out=out_own[tt * 128:(tt + 1) * 128, :], in_=f3[:, 0:D])),
            reads=("f3",), writes=())
    P.end_phase()
    es_f.close()
    es_global.close()
    return nc


def kernel(L0_attn_x, L0_attn_g_ln, L0_attn_W_q, L0_attn_W_k, L0_attn_W_v,
           L0_attn_W_g, L0_attn_W_r, L0_attn_xl_u, L0_attn_xl_v, L0_attn_W_o,
           L0_ffn_w_gate, L0_ffn_w_up, L0_ffn_w_down):
    import ml_dtypes
    x = np.asarray(L0_attn_x, dtype=np.float32)
    B, T, _ = x.shape
    cfg = cfg_for(T)
    NP, NG, OWN = cfg["NP"], cfg["NG"], cfg["OWN"]
    NT0 = NP + NG

    omega = (1.0 / 10000.0 ** (np.arange(0, D, 2, dtype=np.float64) / D)).astype(np.float32)
    omega_c = np.ascontiguousarray(omega.reshape(4, 128).T)
    omega_r = np.ascontiguousarray(omega.reshape(1, 512))
    ident_f = np.eye(128, dtype=np.float32)
    ident_b = ident_f.astype(ml_dtypes.bfloat16)

    common = {
        "omega_c": omega_c, "omega_r": omega_r, "ident_b": ident_b,
        "ident_f": ident_f,
        "gln": np.ascontiguousarray(np.asarray(L0_attn_g_ln, np.float32).reshape(8, 128).T),
        "xlu": np.asarray(L0_attn_xl_u, np.float32).reshape(128, 1),
        "xlv": np.asarray(L0_attn_xl_v, np.float32).reshape(128, 1),
        "Wq": np.asarray(L0_attn_W_q, np.float32),
        "Wk": np.asarray(L0_attn_W_k, np.float32),
        "Wr": np.asarray(L0_attn_W_r, np.float32),
        "Wv": np.asarray(L0_attn_W_v, np.float32),
        "Wg": np.asarray(L0_attn_W_g, np.float32),
        "Wo": np.asarray(L0_attn_W_o, np.float32),
        "wup": np.asarray(L0_ffn_w_up, np.float32),
        "wgate": np.asarray(L0_ffn_w_gate, np.float32),
        "wdown": np.asarray(L0_ffn_w_down, np.float32),
    }

    in_maps = []
    row_maps = []
    for core in range(8):
        s, c = core // 4, core % 4
        rows = []
        for p in range(NP):
            r = rank_of(p, c)
            rows.extend(range(r * CH, (r + 1) * CH))
        rows = np.array(rows, dtype=np.int64)
        row_maps.append((s, rows))
        x_own = np.ascontiguousarray(x[s][rows])
        pos = rows.astype(np.float32)
        mb = np.zeros((128, NP * 4), np.float32)
        for p in range(NP):
            for w in range(2):
                for sub in range(2):
                    posn = pos[p * CH + sub * 128:(p * CH) + (sub + 1) * 128]
                    mb[:, p * 4 + w * 2 + sub] = KT * (2 * p + w) - posn
        t0 = np.zeros((1, NT0), np.float32)
        for p in range(NP):
            t0[0, p] = pos[p * CH]
        for g in range(NG):
            t0[0, NP + g] = g * KT
        m = dict(common)
        m["x_own"] = x_own
        m["x_cov"] = np.ascontiguousarray(x[s])
        m["t0s"] = t0
        m["maskb"] = mb
        in_maps.append(m)

    nc = build_nc(T)
    res = run_bass_kernel_spmd(nc, in_maps, core_ids=list(range(8)))
    out = np.zeros((B, T, D), dtype=np.float32)
    for core in range(8):
        s, rows = row_maps[core]
        out[s][rows] = np.asarray(res.results[core]["out_own"], np.float32)
    return out
